# version 33
# baseline (speedup 1.0000x reference)
import contextlib
import numpy as np
import ml_dtypes
import concourse.bass as bass
import concourse.mybir as mybir
from concourse.bass_utils import run_bass_kernel_spmd

F32 = mybir.dt.float32
BF16 = mybir.dt.bfloat16
I32 = mybir.dt.int32
AF = mybir.ActivationFunctionType
ALU = mybir.AluOpType
AX = mybir.AxisListType

S = 2048
E = 1024
HF = 3072
NH = 16
HD = 64
KSEL = 38
NCS = 16
NCE = 8
NCH = 24
MINVAL = -1.0e30
EPS = 1e-5
QSC = 1.0 / 127.0

ENGS = ("sync", "vector", "scalar", "tensor", "gpsimd")


class Sch:

    def __init__(self, nc):
        self.nc = nc
        self.ops = {e: [] for e in ENGS}
        self.c = {e: 0 for e in ENGS}
        self.d = {}

    def last_tok(self, eng):
        return (eng, "c", self.c[eng]) if self.c[eng] > 0 else None

    def emit(self, eng, fn, deps=(), ms=False, dma=False, ch=0):
        tok = None
        if dma:
            key = (eng, ch)
            self.d[key] = self.d.get(key, 0) + 1
            tok = (eng, ("d", ch), self.d[key] * 16)
        elif ms:
            self.c[eng] += 1
            tok = (eng, "c", self.c[eng])
        self.ops[eng].append((fn, [t for t in deps if t is not None], tok))
        return tok

    def build(self, sems):
        nc = self.nc
        with nc.Block() as block:
            def mk(name):
                def body(eng):
                    hw = {}
                    for fn, deps, tok in self.ops[name]:
                        waits = {}
                        for (de, dk, dv) in deps:
                            key = (de, dk)
                            if waits.get(key, -1) < dv:
                                waits[key] = dv
                        for key, dv in waits.items():
                            if hw.get(key, -1) >= dv:
                                continue
                            eng.wait_ge(sems[key], dv)
                            hw[key] = dv
                        inst = fn(eng)
                        if tok is not None:
                            _, k, v = tok
                            inst.then_inc(sems[(name, k)],
                                          16 if isinstance(k, tuple) else 1)
                return body
            for e in ENGS:
                if self.ops[e]:
                    getattr(block, e)(mk(e))


def build_program():
    nc = bass.Bass("TRN2", target_bir_lowering=False, dynamic_dma_scratch_size=16384)

    x_nat = nc.dram_tensor("x_nat", [S, E], F32, kind="ExternalInput").ap()
    xT = nc.dram_tensor("xT", [E, S], F32, kind="ExternalInput").ap()
    WK2 = nc.dram_tensor("WK2", [NCE, 128, E], F32, kind="ExternalInput").ap()
    WQ2 = nc.dram_tensor("WQ2", [NCE, 128, E], F32, kind="ExternalInput").ap()
    WV2 = nc.dram_tensor("WV2", [128, NCE * E], F32, kind="ExternalInput").ap()
    WO2 = nc.dram_tensor("WO2", [NCE, 128, E], F32, kind="ExternalInput").ap()
    BQKV = nc.dram_tensor("BQKV", [1, 3 * E], F32, kind="ExternalInput").ap()
    BO = nc.dram_tensor("BO", [1, E], F32, kind="ExternalInput").ap()
    WG2 = nc.dram_tensor("WG2", [NCH, 128, E], I32, kind="ExternalInput").ap()
    WU2 = nc.dram_tensor("WU2", [NCH, 128, E], I32, kind="ExternalInput").ap()
    WD2 = nc.dram_tensor("WD2", [NCE, 128, HF], I32, kind="ExternalInput").ap()
    xn_scr = nc.dram_tensor("xn_scr", [S, E], BF16, kind="Internal").ap()
    outT = nc.dram_tensor("outT", [E, S], F32, kind="ExternalOutput").ap()

    id_f32_t = nc.inline_tensor(np.eye(128, dtype=np.float32), "id_f32")
    id_bf_t = nc.inline_tensor(np.eye(128, dtype=ml_dtypes.bfloat16), "id_bf")
    ones_row_t = nc.inline_tensor(np.ones((1, 512), dtype=ml_dtypes.bfloat16), "ones_row")
    ones_col_t = nc.inline_tensor(np.ones((128, 1), dtype=ml_dtypes.bfloat16), "ones_col")
    ones_f32r_t = nc.inline_tensor(np.ones((1, 128), dtype=np.float32), "ones_f32r")
    j38row_t = nc.inline_tensor(
        np.tile(np.arange(KSEL, dtype=np.float32)[None, :], (128, 1)), "j38row")
    j38col_t = nc.inline_tensor(
        np.arange(KSEL, dtype=np.float32)[:, None].copy(), "j38col")

    sch = Sch(nc)
    ctx = contextlib.ExitStack()
    with ctx:
        def sb(name, shape, dt):
            return ctx.enter_context(nc.sbuf_tensor(name, shape, dt)).ap()

        xnT = sb("xnT", [128, NCE * S], BF16)
        AR = sb("AR", [128, 2 * NCE * S], BF16)
        kkT = AR[:, 0:NCE * S]
        vna = AR[:, NCE * S:2 * NCE * S]
        XS = sb("XS", [128, 6144], F32)
        xbuf = [XS[:, 0:1024], XS[:, 1024:2048], XS[:, 2048:3072], XS[:, 3072:4096]]
        sqb2 = [XS[:, 4096:5120], XS[:, 5120:6144]]
        R1 = XS[0:1, 0:2048]
        R2 = XS[0:1, 2048:4096]
        R3 = XS[0:1, 4096:6144]
        rmT_row = XS[0:1, 4096:5312]
        xtb = [XS[:, 3072:3584], XS[:, 3584:4096], XS[:, 4096:4608]]
        obuf = [XS[:, 4608:5120], XS[:, 5120:5632], XS[:, 5632:6144]]
        rank_bf = XS[0:1, 0:1024].bitcast(BF16)
        umask_bf = XS[0:1, 1024:2048].bitcast(BF16)
        WBUF = sb("WBUF", [128, NCE * E], BF16)
        WKQ = sb("WKQ", [128, NCE * E], BF16)
        XNB = sb("XNB", [128, 4096], BF16)
        xnb = [XNB[:, 0:1024], XNB[:, 1024:2048], XNB[:, 2048:3072], XNB[:, 3072:4096]]
        HP8 = sb("HP8", [128, 4 * S], BF16)
        SG = sb("SG", [128, 2 * S], BF16)
        sgb = [SG[:, 0:S], SG[:, S:2 * S]]

        def hbuf(hc):
            if hc < 2:
                return XNB[:, hc * S:(hc + 1) * S]
            if hc < 4:
                return WBUF[:, 4096 + (hc - 2) * S:4096 + (hc - 1) * S]
            if hc < 8:
                return HP8[:, (hc - 4) * S:(hc - 3) * S]
            return AR[:, (hc - 8) * S:(hc - 7) * S]

        qtok = sb("qtok", [128, 512], BF16)
        qT = sb("qT", [128, 512], BF16)
        ctx_sb = XS[0:KSEL, 2048:2560].bitcast(BF16)
        ctxT = sb("ctxT", [128, 512], BF16)
        osel = XS[0:KSEL, 2560:3072].bitcast(BF16)
        EXPT = XS[:, 3072:4096].bitcast(BF16)
        expT = [EXPT[:, 0:1024], EXPT[:, 1024:2048]]
        PtA = sb("PtA", [128, NCS * KSEL], BF16)
        PtT = sb("PtT", [KSEL, S], BF16)
        defbf = sb("defbf", [1, E], BF16)
        bqkv = sb("bqkv", [1, 3 * E], BF16)
        bo_bf = sb("bo_bf", [1, E], BF16)
        rmax32 = sb("rmax32", [KSEL, 2 * NH], F32)
        negb = sb("negb", [128, 1], F32)
        rsb = sb("rsb", [KSEL, NH], F32)
        v16 = sb("v16", [16, 128], F32)
        r16 = sb("r16", [16, 128], F32)
        V128 = sb("V128", [128, NCS], F32)
        m8 = sb("m8", [1, 8], F32)
        mcol = sb("mcol", [128, NCS], F32)
        rcol = sb("rcol", [128, NCS], F32)
        bcol = sb("bcol", [128, NCS], F32)
        sxc = sb("sxc", [128, NCS], F32)
        ssc = sb("ssc", [128, NCS], F32)
        tcol = sb("tcol", [128, NCS], F32)
        msq = sb("msq", [128, NCS], F32)
        id_f = sb("id_f", [128, 128], F32)
        id_b = sb("id_b", [128, 128], BF16)
        ones_r = sb("ones_r", [1, 512], BF16)
        ones_c = sb("ones_c", [128, 1], BF16)
        j38r = sb("j38r", [128, KSEL], F32)
        j38c = sb("j38c", [KSEL, 1], F32)
        ones_fr = sb("ones_fr", [1, 128], F32)
        zrow = sb("zrow", [1, 512], BF16)

        psf = ctx.enter_context(nc.psum_tensor("psf", [128, 3584], F32)).ap()
        psb = ctx.enter_context(nc.psum_tensor("psb", [128, 1024], BF16)).ap()

        em = sch.emit

        em("sync", lambda g: g.dma_start(out=id_f, in_=id_f32_t.ap()), dma=True, ch=0)
        em("sync", lambda g: g.dma_start(out=id_b, in_=id_bf_t.ap()), dma=True, ch=0)
        em("sync", lambda g: g.dma_start(out=ones_r, in_=ones_row_t.ap()), dma=True, ch=0)
        em("sync", lambda g: g.dma_start(out=ones_c, in_=ones_col_t.ap()), dma=True, ch=0)
        em("sync", lambda g: g.dma_start(out=j38r, in_=j38row_t.ap()), dma=True, ch=0)
        em("sync", lambda g: g.dma_start(out=j38c, in_=j38col_t.ap()), dma=True, ch=0)
        t_consts = em("sync", lambda g: g.dma_start(out=ones_fr, in_=ones_f32r_t.ap()), dma=True, ch=0)
        t_id_f = t_id_b = t_ones_r = t_ones_c = t_j38r = t_j38c = t_onesf = t_consts
        t_zrow = em("vector", lambda v: v.memset(zrow, 0.0), ms=True)

        em("gpsimd", lambda g: g.dma_start(out=bqkv, in_=BQKV), dma=True, ch=0)
        t_bo = em("gpsimd", lambda g: g.dma_start(out=bo_bf, in_=BO), dma=True, ch=0)
        t_bqkv = t_bo
        t_wv_h = [None, None]
        for hf in range(2):
            t_wv_h[hf] = em("gpsimd", lambda g, hf=hf: g.dma_start(
                out=WBUF.rearrange("p (k c) -> p k c", k=NCE)[:, :, hf * 512:(hf + 1) * 512],
                in_=WV2.rearrange("p (k c) -> p k c", k=NCE)[:, :, hf * 512:(hf + 1) * 512]),
                dma=True, ch=1)
        t_wkq_h = [None, None]
        for hf in range(2):
            t_wkq_h[hf] = em("gpsimd", lambda g, hf=hf: g.dma_start(
                out=WKQ.rearrange("p (m c) -> p m c", m=NCE)[:, 4 * hf:4 * (hf + 1), :],
                in_=WK2.rearrange("m p c -> p m c")[:, 4 * hf:4 * (hf + 1), :]),
                dma=True, ch=2)

        PKB = [0, 512, 1024, 1536, 2048]
        kst = {"tix": 0}
        t_tile_ev = {}
        t_tile_mm = {}
        kk_toks = []
        v_toks = []
        evq = []

        def drain_ev(upto):
            while evq and evq[0][0] <= upto:
                tix0, fn = evq.pop(0)
                t_tile_ev[tix0] = fn()

        def emit_kk_tile(m, c, dep_tr):
            tix = kst["tix"]; kst["tix"] += 1
            drain_ev(tix - 5)
            pc = PKB[tix % 5]
            last = None
            for k in range(NCE):
                last = em("tensor", lambda t, m=m, c=c, k=k, pc=pc: t.matmul(
                    psf[:, pc:pc + 512],
                    WKQ[:, m * E + k * 128:m * E + (k + 1) * 128],
                    xnT[:, k * S + c * 512:k * S + (c + 1) * 512],
                    start=(k == 0), stop=False),
                    deps=([t_wkq_h[m // 4], t_tile_ev.get(tix - 5)] + list(dep_tr)
                          if k == 0 else []))
            last = em("tensor", lambda t, m=m, pc=pc: t.matmul(
                psf[:, pc:pc + 512],
                bqkv[0:1, E + m * 128:E + (m + 1) * 128],
                ones_r, start=False, stop=True),
                deps=[t_bqkv, t_ones_r], ms=True)
            t_tile_mm[tix] = last
            kk_toks.append(last)
            evq.append((tix, lambda m=m, c=c, pc=pc, last=last: em(
                "scalar", lambda s: s.copy(
                    out=kkT[:, m * S + c * 512:m * S + (c + 1) * 512],
                    in_=psf[:, pc:pc + 512]), deps=[last], ms=True)))

        def emit_v_tile(t_, half, dep_tr):
            tix = kst["tix"]; kst["tix"] += 1
            drain_ev(tix - 5)
            pc = PKB[tix % 5]
            last = None
            for k in range(NCE):
                last = em("tensor", lambda t, t_=t_, half=half, k=k, pc=pc: t.matmul(
                    psf[:, pc:pc + 512],
                    xnT[:, k * S + t_ * 128:k * S + (t_ + 1) * 128],
                    WBUF[:, k * E + half * 512:k * E + (half + 1) * 512],
                    start=(k == 0), stop=False),
                    deps=([t_wv_h[half], t_tile_ev.get(tix - 5)] + list(dep_tr)
                          if k == 0 else []))
            last = em("tensor", lambda t, half=half, pc=pc: t.matmul(
                psf[:, pc:pc + 512],
                ones_r[0:1, 0:128],
                bqkv[0:1, 2 * E + half * 512:2 * E + (half + 1) * 512],
                start=False, stop=True),
                deps=[t_bqkv], ms=True)
            t_tile_mm[tix] = last
            v_toks.append(last)
            evq.append((tix, lambda t_=t_, half=half, pc=pc, last=last: em(
                "scalar", lambda s: s.copy(
                    out=vna[:, t_ * E + half * 512:t_ * E + (half + 1) * 512],
                    in_=psf[:, pc:pc + 512]), deps=[last], ms=True)))

        t_dx = [None] * NCS
        t_rsx = [None] * NCS
        t_rss = [None] * NCS
        t_xn = [None] * NCS
        t_sq = [None] * NCS
        t_tr = [None] * NCS
        t_scrw = [None] * NCS
        t_trev = [None] * NCS
        t_vcol = [None] * NCS
        t_rcolt = [None] * NCS
        t_bcol = [None] * NCS

        def ln_dx(i):
            frees = [t_xn[i - 4] if i >= 4 else None,
                     t_rsx[i - 4] if i >= 4 else None,
                     t_rss[i - 4] if i >= 4 else None]
            t_dx[i] = [em("sync", lambda g, i=i, q=q: g.dma_start(
                out=xbuf[i % 4][q * 32:(q + 1) * 32, :],
                in_=x_nat[i * 128 + q * 32:i * 128 + (q + 1) * 32, :]),
                deps=frees, dma=True, ch=1 + 4 * (i % 2) + q)
                for q in range(4)]

        for i in range(2):
            ln_dx(i)
        for it in range(NCS + 2):
            i = it
            if i < NCS:
                sqb = sqb2[i % 2]
                t_sq[i] = em("scalar", lambda s, i=i, sqb=sqb: s.activation(
                    out=sqb, in_=xbuf[i % 4], func=AF.Square),
                    deps=t_dx[i] + [t_rss[i - 2] if i >= 2 else None], ms=True)
            if 1 <= it <= NCS:
                j = it - 1
                t_ln = em("scalar", lambda s, j=j: s.activation(
                    out=tcol[:, j:j + 1], in_=V128[:, j:j + 1], func=AF.Ln),
                    deps=[t_vcol[j]], ms=True)
                t_rcolt[j] = em("scalar", lambda s, j=j: s.activation(
                    out=rcol[:, j:j + 1], in_=tcol[:, j:j + 1], func=AF.Exp, scale=-0.5),
                    deps=[t_ln], ms=True)
            if 2 <= it:
                j = it - 2
                t_xn[j] = em("scalar", lambda s, j=j: s.activation(
                    out=xnb[j % 2], in_=xbuf[j % 4], func=AF.Identity,
                    bias=bcol[:, j:j + 1], scale=rcol[:, j:j + 1]),
                    deps=t_dx[j] + [t_rcolt[j], t_bcol[j],
                          t_tr[j - 2] if j >= 2 else None,
                          t_scrw[j - 2] if j >= 2 else None], ms=True)
            if i < NCS:
                sqb = sqb2[i % 2]
                t_rsx[i] = em("vector", lambda v, i=i: v.reduce_sum(
                    out=sxc[:, i:i + 1], in_=xbuf[i % 4], axis=AX.X),
                    deps=t_dx[i], ms=True)
                t_rss[i] = em("vector", lambda v, i=i, sqb=sqb: v.reduce_sum(
                    out=ssc[:, i:i + 1], in_=sqb, axis=AX.X),
                    deps=[t_sq[i]], ms=True)
                t_m = em("vector", lambda v, i=i: v.tensor_scalar(
                    out=mcol[:, i:i + 1], in0=sxc[:, i:i + 1], scalar1=1.0 / E,
                    scalar2=None, op0=ALU.mult), deps=[t_rsx[i]], ms=True)
                t_msq = em("vector", lambda v, i=i: v.tensor_scalar(
                    out=msq[:, i:i + 1], in0=mcol[:, i:i + 1],
                    scalar1=mcol[:, i:i + 1], scalar2=-EPS,
                    op0=ALU.mult, op1=ALU.add), deps=[t_m], ms=True)
                t_vcol[i] = em("vector", lambda v, i=i: v.scalar_tensor_tensor(
                    out=V128[:, i:i + 1], in0=ssc[:, i:i + 1], scalar=1.0 / E,
                    in1=msq[:, i:i + 1], op0=ALU.mult, op1=ALU.subtract),
                    deps=[t_rss[i], t_msq], ms=True)
            if 1 <= it <= NCS:
                j = it - 1
                t_bcol[j] = em("vector", lambda v, j=j: v.scalar_tensor_tensor(
                    out=bcol[:, j:j + 1], in0=mcol[:, j:j + 1], scalar=-1.0,
                    in1=rcol[:, j:j + 1], op0=ALU.mult, op1=ALU.mult),
                    deps=[t_rcolt[j]], ms=True)
            if 2 <= it:
                j = it - 2
                tr_last = None
                for e in range(NCE):
                    tr_last = em("tensor", lambda t, j=j, e=e: t.transpose(
                        psb[:, e * 128:(e + 1) * 128], xnb[j % 2][:, e * 128:(e + 1) * 128],
                        id_b), deps=[t_xn[j], t_id_b,
                                     t_trev[j - 1] if j >= 1 else None],
                        ms=(e == NCE - 1))
                if j % 2 == 0:
                    t_trev[j] = em("scalar", lambda s, j=j: s.copy(
                        out=xnT.rearrange("p (k s) -> p k s", k=NCE)[:, :, j * 128:(j + 1) * 128],
                        in_=psb.rearrange("p (k s) -> p k s", k=NCE)),
                        deps=[tr_last], ms=True)
                else:
                    t_trev[j] = em("vector", lambda v, j=j: v.tensor_copy(
                        out=xnT.rearrange("p (k s) -> p k s", k=NCE)[:, :, j * 128:(j + 1) * 128],
                        in_=psb.rearrange("p (k s) -> p k s", k=NCE)),
                        deps=[tr_last], ms=True)
                for n in range(2):
                    t_tr[j] = em("tensor", lambda t, j=j, n=n: t.matmul(
                        psf[0:1, 2560 + n * 512:2560 + (n + 1) * 512],
                        ones_c, xnb[j % 2][:, n * 512:(n + 1) * 512],
                        start=(j == 0), stop=(j == NCS - 1)),
                        deps=[t_xn[j], t_ones_c], ms=(n == 1))
                t_scrw[j] = em("sync", lambda g, j=j: g.dma_start(
                    out=xn_scr[j * 128:(j + 1) * 128, :], in_=xnb[j % 2]),
                    deps=[t_xn[j]], dma=True, ch=9 + j % 2)
                if j % 4 == 3:
                    for m in range(NCE):
                        emit_kk_tile(m, j // 4, [t_trev[j], t_trev[j - 1]])
            if 2 <= it + 2 < NCS:
                ln_dx(it + 2)

        t_def = em("vector", lambda v: v.tensor_scalar(
            out=defbf, in0=psf[0:1, 2560:3584], scalar1=1.0 / S, scalar2=None,
            op0=ALU.mult), deps=[t_tr[NCS - 1]], ms=True)

        t_vtr = em("tensor", lambda t: t.transpose(
            psf[0:16, 3072:3200], V128, id_f),
            deps=[t_vcol[NCS - 1], t_id_f, t_def], ms=True)
        t_v16 = em("vector", lambda v: v.tensor_copy(out=v16, in_=psf[0:16, 3072:3200]),
                   deps=[t_vtr, t_def], ms=True)
        t_vrow = em("sync", lambda g: g.dma_start(out=R1, in_=v16),
                    deps=[t_v16, t_xn[NCS - 1], t_rsx[NCS - 1], t_rss[NCS - 1]],
                    dma=True, ch=15)
        prev = t_vrow
        for r in range(5):
            t_mx = em("vector", lambda v: v.max(out=m8, in_=R1), deps=[prev], ms=True)
            if r == 4:
                t_mx = em("vector", lambda v: v.memset(m8[:, 6:8], MINVAL),
                          deps=[t_mx], ms=True)
            prev = em("vector", lambda v: v.match_replace(
                out=R1, in_to_replace=m8, in_values=R1, imm_value=MINVAL),
                deps=[t_mx], ms=True)
        t_mask = em("vector", lambda v: v.tensor_scalar(
            out=R2, in0=R1, scalar1=-1.0e29, scalar2=None, op0=ALU.is_le),
            deps=[prev, t_rss[NCS - 1]], ms=True)
        t_z = em("vector", lambda v: v.memset(R1, 0.0), deps=[t_mask], ms=True)
        t_scan = em("vector", lambda v: v.tensor_tensor_scan(
            out=R3, data0=R2, data1=R1, initial=0.0, op0=ALU.add, op1=ALU.add),
            deps=[t_mask, t_z], ms=True)
        t_rk = em("vector", lambda v: v.scalar_tensor_tensor(
            out=R2, in0=R2, scalar=-51.0, in1=R3, op0=ALU.mult, op1=ALU.add),
            deps=[t_scan], ms=True)
        t_rk = em("vector", lambda v: v.tensor_scalar(
            out=R2, in0=R2, scalar1=50.0, scalar2=None, op0=ALU.add),
            deps=[t_rk], ms=True)
        t_rkbf = em("vector", lambda v: v.tensor_copy(out=rank_bf, in_=R2),
                    deps=[t_rk, t_z], ms=True)
        t_um = em("vector", lambda v: v.tensor_scalar(
            out=umask_bf, in0=R2, scalar1=49.0, scalar2=None, op0=ALU.is_ge),
            deps=[t_rk, t_rkbf], ms=True)
        t_r16 = em("sync", lambda g: g.dma_start(out=r16, in_=R2),
                   deps=[t_rk, t_um], dma=True, ch=15)

        for t_ in range(NCS):
            for half in range(2):
                emit_v_tile(t_, half, [t_trev[t_]])
        drain_ev(kst["tix"])
        t_pA = em("vector", lambda v: v.engine_nop(),
                  deps=[t_tile_ev[kst["tix"] - 1]], ms=True)
        t_kk_all_mm = kk_toks[-1]
        t_v_all_mm = v_toks[-1]
        t_kk_all_ev = t_v_all_ev = t_pA

        t_ptT = [None] * 4
        prev_ev = t_v16
        for c in range(4):
            t_mm = em("tensor", lambda t, c=c: t.matmul(
                psf[0:KSEL, 3072:3584], ones_r[0:1, 0:KSEL],
                rank_bf[0:1, c * 512:(c + 1) * 512], start=True, stop=True),
                deps=[t_rkbf, t_ones_r, prev_ev], ms=True)
            prev_ev = em("vector", lambda v, c=c: v.tensor_scalar(
                out=PtT[:, c * 512:(c + 1) * 512], in0=psf[0:KSEL, 3072:3584],
                scalar1=j38c, scalar2=None, op0=ALU.is_equal),
                deps=[t_mm, t_j38c], ms=True)
            t_ptT[c] = prev_ev
        t_rtr = em("tensor", lambda t: t.transpose(
            psf[:, 3072:3088], r16, id_f[0:16, 0:16]),
            deps=[t_r16, prev_ev], ms=True)
        t_r128 = em("vector", lambda v: v.tensor_copy(
            out=V128, in_=psf[:, 3072:3088]), deps=[t_rtr], ms=True)
        t_pta = [None] * NCS
        for t_ in range(NCS):
            t_pta[t_] = em("vector", lambda v, t_=t_: v.tensor_scalar(
                out=PtA[:, t_ * KSEL:(t_ + 1) * KSEL], in0=j38r,
                scalar1=V128[:, t_:t_ + 1], scalar2=None, op0=ALU.is_equal),
                deps=[t_r128, t_j38r], ms=True)

        t_kn = [None] * NCE
        t_sqk = [None] * NCE

        def emit_kn(m):
            if m % 2 == 0:
                t_sqk[m] = em("scalar", lambda s, m=m: s.activation(
                    out=SG[:, 0:S], in_=kkT[:, m * S:(m + 1) * S], func=AF.Square),
                    deps=[t_kk_all_ev, t_kn[m - 1] if m >= 1 else None], ms=True)
            else:
                t_sqk[m] = em("vector", lambda v, m=m: v.tensor_tensor(
                    out=SG[:, 0:S], in0=kkT[:, m * S:(m + 1) * S],
                    in1=kkT[:, m * S:(m + 1) * S], op=ALU.mult),
                    deps=[t_kk_all_ev, t_kn[m - 1] if m >= 1 else None], ms=True)
            last = None
            for cc in range(4):
                last = em("tensor", lambda t, m=m, cc=cc: t.matmul(
                    psf[0:1, cc * 512:(cc + 1) * 512],
                    ones_c, SG[:, cc * 512:(cc + 1) * 512],
                    start=(m == 0), stop=(m == NCE - 1)),
                    deps=[t_sqk[m], t_v_all_ev] if cc == 0 else [],
                    ms=(cc == 3))
            t_kn[m] = last

        t_gz = em("tensor", lambda t: t.matmul(
            psf[:, 3072:3584], ones_r[0:1, 0:128], zrow,
            start=True, stop=False),
            deps=[t_zrow, t_r128, t_v_all_ev], ms=True)
        t_xnr = [None] * NCS
        t_gmm = [None] * NCS
        for i in range(NCS):
            b = i % 4
            t_xnr[i] = em("sync", lambda g, i=i, b=b: g.dma_start(
                out=xnb[b], in_=xn_scr[i * 128:(i + 1) * 128, :]),
                deps=[t_scrw[NCS - 1], t_scrw[NCS - 2],
                      t_gmm[i - 4] if i >= 4 else None,
                      (t_tr[14 + i] if i < 2 else None)],
                dma=True, ch=7 + i % 4)
            last = None
            for e in range(NCE):
                last = em("tensor", lambda t, i=i, b=b, e=e: t.matmul(
                    psf[:, 3072 + e * 64:3072 + e * 64 + KSEL],
                    xnb[b][:, e * 128:(e + 1) * 128],
                    PtA[:, i * KSEL:(i + 1) * KSEL],
                    start=False,
                    stop=(i == NCS - 1)),
                    deps=[t_xnr[i], t_pta[i], t_r128, t_v_all_ev],
                    ms=(e == NCE - 1))
            t_gmm[i] = last
            if i % 2 == 1 and i // 2 < NCE:
                emit_kn(i // 2)
        t_qev = em("vector", lambda v: v.tensor_copy(
            out=qtok.rearrange("p (m j) -> p m j", m=NCE)[:, :, 0:KSEL],
            in_=psf[:, 3072:3584].rearrange("p (m j) -> p m j", m=NCE)[:, :, 0:KSEL]),
            deps=[t_gmm[NCS - 1]], ms=True)

        t_wqf = em("gpsimd", lambda g: g.dma_start(
            out=WKQ.rearrange("p (m c) -> p m c", m=NCE),
            in_=WQ2.rearrange("m p c -> p m c")),
            deps=[t_kk_all_mm], dma=True, ch=2)
        t_qTmm = [None] * NCE
        for m in range(NCE):
            last = None
            for k in range(NCE):
                last = em("tensor", lambda t, m=m, k=k: t.matmul(
                    psf[:, 3072 + m * 64:3072 + m * 64 + KSEL],
                    WKQ[:, m * E + k * 128:m * E + (k + 1) * 128],
                    qtok[:, k * 64:k * 64 + KSEL],
                    start=(k == 0), stop=False),
                    deps=[t_wqf, t_qev])
            last = em("tensor", lambda t, m=m: t.matmul(
                psf[:, 3072 + m * 64:3072 + m * 64 + KSEL],
                bqkv[0:1, m * 128:(m + 1) * 128],
                ones_r[0:1, 0:KSEL], start=False, stop=(m == NCE - 1)),
                deps=[t_bqkv], ms=True)
            t_qTmm[m] = last
        t_qTev = em("vector", lambda v: v.tensor_copy(
            out=qT.rearrange("p (m j) -> p m j", m=NCE)[:, :, 0:KSEL],
            in_=psf[:, 3072:3584].rearrange("p (m j) -> p m j", m=NCE)[:, :, 0:KSEL]),
            deps=[t_qTmm[NCE - 1]], ms=True)
        t_qpad = em("vector", lambda v: v.memset(
            qT.rearrange("p (m j) -> p m j", m=NCE)[:, :, KSEL:64], 0.0),
            deps=[t_qTev], ms=True)

        t_kn2 = em("vector", lambda v: v.reduce_max(
            out=negb[0:1, 0:1], in_=psf[0:1, 0:2048], axis=AX.X),
            deps=[t_kn[NCE - 1]], ms=True)
        t_sqq = em("scalar", lambda s: s.activation(
            out=SG[:, 0:512], in_=qT, func=AF.Square),
            deps=[t_qpad, t_qTev, t_kn[NCE - 1]], ms=True)
        t_qn = em("tensor", lambda t: t.matmul(
            psf[0:1, 3072:3584], ones_c, SG[:, 0:512], start=True, stop=True),
            deps=[t_sqq, t_qTmm[NCE - 1], t_qev], ms=True)
        t_qn2 = em("vector", lambda v: v.reduce_sum(
            out=v16[0:1, 0:64],
            in_=psf[0:1, 3072:3584].rearrange("a (m j) -> a j m", m=NCE),
            axis=AX.X), deps=[t_qn], ms=True)
        t_qn3 = em("vector", lambda v: v.reduce_max(
            out=v16[0:1, 64:65], in_=v16[0:1, 0:KSEL], axis=AX.X),
            deps=[t_qn2], ms=True)
        t_b2 = em("vector", lambda v: v.tensor_tensor(
            out=v16[0:1, 65:66], in0=v16[0:1, 64:65], in1=negb[0:1, 0:1],
            op=ALU.mult), deps=[t_qn3, t_kn2], ms=True)
        t_bsq = em("scalar", lambda s: s.activation(
            out=v16[0:1, 66:67], in_=v16[0:1, 65:66], func=AF.Sqrt),
            deps=[t_b2], ms=True)
        t_negb8 = em("vector", lambda v: v.tensor_scalar(
            out=v16[0:1, 67:68], in0=v16[0:1, 66:67], scalar1=-1.02 / 8.0,
            scalar2=None, op0=ALU.mult), deps=[t_bsq], ms=True)
        t_bb = em("tensor", lambda t: t.matmul(
            psf[:, 3110:3111], ones_fr, v16[0:1, 67:68], start=True, stop=True),
            deps=[t_negb8, t_onesf, t_qn2], ms=True)
        t_negb = em("vector", lambda v: v.tensor_copy(
            out=negb, in_=psf[:, 3110:3111]), deps=[t_bb], ms=True)

        SCB = [2048, 2560, 2048 + 256, 2560 + 256]

        def sc_col(c):
            g = c // 4
            return 2048 + (g % 2) * 512 + (g // 2) * 256 + (c % 4) * 64

        t_exp = {}
        t_ctxev = [None] * NH
        t_sslast = [None] * NH
        t_ctxlast = [None] * NH
        for h in range(NH):
            po = (h % 2) * 64
            mo = h // 2
            eb = expT[h % 2]
            t_ss = t_cx = None
            for g in range(4):
                for c in range(4 * g, 4 * g + 4):
                    em("tensor", lambda t, h=h, c=c, po=po, mo=mo: t.matmul(
                        psf[:, sc_col(c):sc_col(c) + KSEL],
                        kkT[po:po + 64, mo * S + c * 128:mo * S + (c + 1) * 128],
                        qT[po:po + 64, mo * 64:mo * 64 + KSEL],
                        start=True, stop=True),
                        deps=[t_exp.get((h, g - 2)) or t_exp.get((h - 1, 2 + g)),
                              t_ctxlast[h - 2] if h >= 2 else None,
                              t_sslast[h - 2] if h >= 2 else None,
                              t_def if h == 0 else None,
                              t_v_all_ev if h == 0 else None],
                        ms=(c % 4 == 3))
                gcol = 2048 + (g % 2) * 512 + (g // 2) * 256
                t_exp[(h, g)] = em("scalar", lambda s, h=h, g=g, eb=eb, gcol=gcol:
                    s.activation(
                        out=eb[:, g * 256:(g + 1) * 256].rearrange(
                            "p (c j) -> p c j", c=4)[:, :, 0:KSEL],
                        in_=psf[:, gcol:gcol + 256].rearrange(
                            "p (c j) -> p c j", c=4)[:, :, 0:KSEL],
                        func=AF.Exp, scale=0.125, bias=negb),
                    deps=[sch.last_tok("tensor"), t_negb,
                          t_sslast[h - 2] if h >= 2 else None,
                          t_ctxlast[h - 2] if h >= 2 else None], ms=True)
                if g >= 1:
                    for c in range(4 * (g - 1), 4 * g):
                        t_ss = em("tensor", lambda t, h=h, c=c, eb=eb: t.matmul(
                            psf[0:KSEL, 3072 + 2 * h:3072 + 2 * h + 1],
                            eb[:, c * 64:c * 64 + KSEL], ones_c,
                            start=(c == 0), stop=False),
                            deps=[t_exp[(h, c // 4)],
                                  t_negb if (h == 0 and c == 0) else None])
                        t_cx = em("tensor", lambda t, h=h, c=c, eb=eb: t.matmul(
                            psf[0:KSEL, 1024 + h * 64:1024 + (h + 1) * 64],
                            eb[:, c * 64:c * 64 + KSEL],
                            vna[:, c * E + h * 64:c * E + (h + 1) * 64],
                            start=(c == 0), stop=False),
                            deps=[t_exp[(h, c // 4)],
                                  t_kn2 if (h == 0 and c == 0) else None])
            for c in range(12, 16):
                t_ss = em("tensor", lambda t, h=h, c=c, eb=eb: t.matmul(
                    psf[0:KSEL, 3072 + 2 * h:3072 + 2 * h + 1],
                    eb[:, c * 64:c * 64 + KSEL], ones_c,
                    start=False, stop=(c == NCS - 1)),
                    deps=[t_exp[(h, 3)]], ms=(c == NCS - 1))
                t_cx = em("tensor", lambda t, h=h, c=c, eb=eb: t.matmul(
                    psf[0:KSEL, 1024 + h * 64:1024 + (h + 1) * 64],
                    eb[:, c * 64:c * 64 + KSEL],
                    vna[:, c * E + h * 64:c * E + (h + 1) * 64],
                    start=False, stop=(c == NCS - 1)),
                    deps=[t_exp[(h, 3)]], ms=(c == NCS - 1))
            t_sslast[h] = t_ss
            t_ctxlast[h] = t_cx
        em("scalar", lambda s: s.activation(
            out=tcol[:, 0:1], in_=tcol[:, 0:1], func=AF.Silu),
            deps=[t_exp[(NH - 1, 3)]], ms=True)
        for h in range(NH):
            t_rc = em("vector", lambda v, h=h: v.reciprocal(
                out=rsb[:, h:h + 1], in_=psf[0:KSEL, 3072 + 2 * h:3072 + 2 * h + 1]),
                deps=[t_sslast[NH - 1], t_ctxlast[NH - 1]], ms=True)
            t_ctxev[h] = em("scalar", lambda s, h=h: s.activation(
                out=ctx_sb[:, h * 64:(h + 1) * 64],
                in_=psf[0:KSEL, 1024 + h * 64:1024 + (h + 1) * 64],
                func=AF.Identity, scale=rsb[:, h:h + 1]),
                deps=[t_ctxlast[NH - 1], t_sslast[NH - 1], t_rc], ms=True)

        tr_last = None
        for k in range(NCE):
            tr_last = em("tensor", lambda t, k=k: t.transpose(
                psb[:, k * 64:k * 64 + KSEL],
                ctx_sb[:, k * 128:(k + 1) * 128], id_b[0:KSEL, 0:KSEL]),
                deps=[t_ctxev[2 * k], t_ctxev[2 * k + 1],
                      t_trev[NCS - 1]], ms=(k == NCE - 1))
        t_ctxT = em("vector", lambda v: v.tensor_copy(
            out=ctxT, in_=psb[:, 0:512]), deps=[tr_last], ms=True)
        t_wo = em("gpsimd", lambda g: g.dma_start(
            out=WKQ.rearrange("p (k j) -> p k j", k=NCE),
            in_=WO2.rearrange("k p j -> p k j")),
            deps=[t_qTmm[NCE - 1]], dma=True, ch=1)
        t_oselmm = [None] * 2
        for n in range(2):
            last = None
            for k in range(NCE):
                last = em("tensor", lambda t, n=n, k=k: t.matmul(
                    psf[0:KSEL, 2048 + n * 512:2048 + (n + 1) * 512],
                    ctxT[:, k * 64:k * 64 + KSEL],
                    WKQ[:, k * E + n * 512:k * E + (n + 1) * 512],
                    start=(k == 0), stop=False),
                    deps=[t_ctxT, t_wo,
                          t_sslast[NH - 1], t_ctxlast[NH - 1]])
            last = em("tensor", lambda t, n=n: t.matmul(
                psf[0:KSEL, 2048 + n * 512:2048 + (n + 1) * 512],
                ones_r[0:1, 0:KSEL], bo_bf[0:1, n * 512:(n + 1) * 512],
                start=False, stop=True), deps=[t_bo], ms=True)
            t_oselmm[n] = last
        t_oselev = em("scalar", lambda s: s.copy(out=osel, in_=psf[0:KSEL, 2048:3072]),
                      deps=[t_oselmm[0], t_oselmm[1]], ms=True)

        t_stprev = {}
        t_dnall = {}
        t_wg = {}
        t_wu = {}
        t_silu = {}
        t_hp = {}
        t_glast = {}
        t_ulast = {}
        t_wd = {}
        t_dnev = {}
        t_f2last = {}

        def h_war(hc, cc):
            if cc != 0:
                return []
            if hc == 0:
                return [t_gmm[NCS - 1], t_qev, t_oselev]
            if hc == 1:
                return [t_gmm[NCS - 1]]
            if hc == 2:
                return [t_v_all_mm]
            if hc == 8:
                return [t_sslast[NH - 1], t_kn[NCE - 1]]
            if hc == 16:
                return [t_ctxlast[NH - 1]]
            return []

        for hc in range(NCH):
            wslot = hc % 2
            t_wg[hc] = em("gpsimd", lambda g, hc=hc, wslot=wslot:
                g.dma_start(out=WBUF[:, wslot * 1024:(wslot + 1) * 1024],
                            in_=WG2[hc]),
                deps=[t_glast.get(hc - 2),
                      (t_v_all_mm if hc < 2 else None)],
                dma=True, ch=3 + hc % 2)
            t_wu[hc] = em("gpsimd", lambda g, hc=hc, wslot=wslot:
                g.dma_start(out=WBUF[:, 2048 + wslot * 1024:2048 + (wslot + 1) * 1024],
                            in_=WU2[hc]),
                deps=[t_ulast.get(hc - 2),
                      (t_v_all_mm if hc < 2 else None)],
                dma=True, ch=5 + hc % 2)
            gl = {}
            for cc in range(4):
                p2 = (hc, cc - 2) if cc >= 2 else (hc - 1, cc + 2)
                gco = (cc % 2) * 512
                uco = 1024 + (cc % 2) * 512
                glast = None
                for k in range(NCE):
                    glast = em("tensor", lambda t, hc=hc, cc=cc, k=k,
                               wslot=wslot, gco=gco: t.matmul(
                        psf[:, gco:gco + 512],
                        WBUF[:, wslot * 1024 + k * 128:wslot * 1024 + (k + 1) * 128],
                        xnT[:, k * S + cc * 512:k * S + (cc + 1) * 512],
                        start=(k == 0), stop=(k == NCE - 1)),
                        deps=([t_wg[hc], t_silu.get(p2)] +
                              ([t_exp[(NH - 1, 3)], t_kn2] if (hc == 0 and cc < 2) else [])
                              if k == 0 else []),
                        ms=(k == NCE - 1))
                gl[cc] = glast
                if cc == 3:
                    t_glast[hc] = glast
                ulast = None
                for k in range(NCE):
                    ulast = em("tensor", lambda t, hc=hc, cc=cc, k=k,
                               wslot=wslot, uco=uco: t.matmul(
                        psf[:, uco:uco + 512],
                        WBUF[:, 2048 + wslot * 1024 + k * 128:2048 + wslot * 1024 + (k + 1) * 128],
                        xnT[:, k * S + cc * 512:k * S + (cc + 1) * 512],
                        start=(k == 0), stop=(k == NCE - 1)),
                        deps=([t_wu[hc], t_hp.get(p2)] +
                              ([t_ctxev[NH - 1], t_kn2] if (hc == 0 and cc < 2) else [])
                              if k == 0 else []),
                        ms=(k == NCE - 1))
                if cc == 3:
                    t_ulast[hc] = ulast
                t_silu[(hc, cc)] = em("scalar",
                    lambda s, hc=hc, cc=cc, gco=gco: s.activation(
                        out=sgb[hc % 2][:, cc * 512:(cc + 1) * 512],
                        in_=psf[:, gco:gco + 512],
                        func=AF.Silu, scale=QSC),
                    deps=[gl[cc], ulast, t_hp.get((hc - 2, cc)),
                          (t_kn[NCE - 1] if hc == 0 else None),
                          (t_qn if hc == 0 else None)], ms=True)
                t_hp[(hc, cc)] = em("vector",
                    lambda v, hc=hc, cc=cc, uco=uco: v.scalar_tensor_tensor(
                        out=hbuf(hc)[:, cc * 512:(cc + 1) * 512],
                        in0=psf[:, uco:uco + 512],
                        scalar=QSC * QSC,
                        in1=sgb[hc % 2][:, cc * 512:(cc + 1) * 512],
                        op0=ALU.mult, op1=ALU.mult),
                    deps=[t_silu[(hc, cc)], ulast] + h_war(hc, cc),
                    ms=True)
        for m in range(NCE):
            wdo = (m % 2) * 3072
            t_wd[m] = em("gpsimd", lambda g, m=m, wdo=wdo: g.dma_start(
                    out=WKQ[:, wdo:wdo + 3072], in_=WD2[m]),
                deps=[t_f2last.get((m - 2, 3)),
                      (t_oselmm[1] if m < 2 else None)],
                dma=True, ch=7 + m % 2)
            for cc in range(4):
                tile_i = m * 4 + cc
                par = tile_i % 3
                pcol = 2048 + (tile_i % 3) * 512
                last = None
                for hc in range(NCH):
                    last = em("tensor", lambda t, m=m, cc=cc, hc=hc,
                              pcol=pcol, wdo=wdo: t.matmul(
                        psf[:, pcol:pcol + 512],
                        WKQ[:, wdo + hc * 128:wdo + (hc + 1) * 128],
                        hbuf(hc)[:, cc * 512:(cc + 1) * 512],
                        start=(hc == 0), stop=False),
                        deps=([t_wd[m], t_hp[(NCH - 1, cc)],
                               t_dnall.get(tile_i - 3)]
                              if hc == 0 else []))
                last = em("tensor", lambda t, m=m, cc=cc, pcol=pcol: t.matmul(
                    psf[:, pcol:pcol + 512],
                    osel[:, m * 128:(m + 1) * 128],
                    PtT[:, cc * 512:(cc + 1) * 512],
                    start=False, stop=False),
                    deps=[t_oselev, t_ptT[cc]])
                last = em("tensor", lambda t, m=m, cc=cc, pcol=pcol: t.matmul(
                    psf[:, pcol:pcol + 512],
                    defbf[0:1, m * 128:(m + 1) * 128],
                    umask_bf[0:1, cc * 512:(cc + 1) * 512],
                    start=False, stop=True),
                    deps=[t_def, t_um], ms=True)
                t_f2last[(m, cc)] = last
                t_xt = em("sync", lambda g, m=m, cc=cc, par=par: g.dma_start(
                    out=xtb[par],
                    in_=xT[m * 128:(m + 1) * 128, cc * 512:(cc + 1) * 512]),
                    deps=[t_um if tile_i == 0 else None,
                          t_r16 if tile_i == 0 else None,
                          t_ctxlast[NH - 1] if tile_i < 2 else None,
                          t_dnall.get(tile_i - 3)],
                    dma=True, ch=11 + par)
                t_dnev[(m, cc)] = em("vector",
                    lambda v, par=par, pcol=pcol: v.tensor_tensor(
                        out=obuf[par], in0=psf[:, pcol:pcol + 512],
                        in1=xtb[par], op=ALU.add),
                    deps=[last, t_xt, t_stprev.get(par)], ms=True)
                t_dnall[tile_i] = t_dnev[(m, cc)]
                t_stprev[par] = em("sync", lambda g, m=m, cc=cc, par=par:
                    g.dma_start(
                        out=outT[m * 128:(m + 1) * 128, cc * 512:(cc + 1) * 512],
                        in_=obuf[par]),
                    deps=[t_dnev[(m, cc)]], dma=True, ch=13 + par)

        em("sync", lambda g: g.nop(),
           deps=[t_stprev.get(0), t_stprev.get(1), t_stprev.get(2)])

        sems = {}
        for e in ENGS:
            sems[(e, "c")] = ctx.enter_context(nc.semaphore(f"s_{e}_c"))
        for (e, chn) in sorted(sch.d):
            sems[(e, ("d", chn))] = ctx.enter_context(nc.semaphore(f"s_{e}_d{chn}"))
        sch.build(sems)
    return nc


_NC_CACHE = {}


def _get_nc():
    if "nc" not in _NC_CACHE:
        _NC_CACHE["nc"] = build_program()
    return _NC_CACHE["nc"]


def kernel(L0_attn_x, L0_attn_in_proj_w, L0_attn_in_proj_b, L0_attn_out_w,
           L0_attn_out_b, L0_ffn_w_gate, L0_ffn_w_up, L0_ffn_w_down):
    x = np.asarray(L0_attn_x)
    ipw = np.asarray(L0_attn_in_proj_w)
    ipb = np.asarray(L0_attn_in_proj_b)
    ow = np.asarray(L0_attn_out_w)
    ob = np.asarray(L0_attn_out_b)
    wg = np.asarray(L0_ffn_w_gate)
    wu = np.asarray(L0_ffn_w_up)
    wd = np.asarray(L0_ffn_w_down)

    Wq, Wk, Wv = ipw[0:E], ipw[E:2 * E], ipw[2 * E:3 * E]
    def blk_kM(W):
        return np.ascontiguousarray(
            W.reshape(NCE, 128, NCE, 128).transpose(0, 3, 2, 1).reshape(NCE, 128, E))
    WK2 = blk_kM(Wk)
    WQ2 = blk_kM(Wq)
    WV2 = np.ascontiguousarray(
        Wv.reshape(E, NCE, 128).transpose(2, 1, 0).reshape(128, NCE * E))
    WO2 = np.ascontiguousarray(
        ow.reshape(E, NCE, 128).transpose(1, 2, 0).reshape(NCE, 128, E))
    WG2 = np.ascontiguousarray(
        wg.reshape(NCH, 128, NCE, 128).transpose(0, 3, 2, 1).reshape(NCH, 128, E))
    WU2 = np.ascontiguousarray(
        wu.reshape(NCH, 128, NCE, 128).transpose(0, 3, 2, 1).reshape(NCH, 128, E))
    WD2 = np.ascontiguousarray(
        wd.reshape(NCE, 128, NCH, 128).transpose(0, 3, 2, 1).reshape(NCE, 128, HF))
    BQKV = np.ascontiguousarray(ipb.reshape(1, 3 * E))
    BOa = np.ascontiguousarray(ob.reshape(1, E))

    nc = _get_nc()
    in_maps = []
    for b in range(8):
        in_maps.append(dict(
            x_nat=np.ascontiguousarray(x[b]),
            xT=np.ascontiguousarray(x[b].T),
            WK2=WK2, WQ2=WQ2, WV2=WV2, WO2=WO2, BQKV=BQKV, BO=BOa,
            WG2=WG2, WU2=WU2, WD2=WD2,
        ))
    res = run_bass_kernel_spmd(nc, in_maps, core_ids=list(range(8)))
    out = np.stack([np.asarray(res.results[b]["outT"]).T for b in range(8)], axis=0)
    return np.ascontiguousarray(out, dtype=np.float32)


# version 38
# speedup vs baseline: 1.1755x; 1.1755x over previous
import contextlib
import numpy as np
import ml_dtypes
import concourse.bass as bass
import concourse.mybir as mybir
from concourse.bass_utils import run_bass_kernel_spmd

F32 = mybir.dt.float32
BF16 = mybir.dt.bfloat16
I32 = mybir.dt.int32
AF = mybir.ActivationFunctionType
ALU = mybir.AluOpType
AX = mybir.AxisListType

S = 2048
E = 1024
HF = 3072
NH = 16
HD = 64
KSEL = 38
NCS = 16
NCE = 8
NCH = 24
MINVAL = -1.0e30
EPS = 1e-5
QSC = 1.0 / 127.0

ENGS = ("sync", "vector", "scalar", "tensor", "gpsimd")


class Sch:

    def __init__(self, nc):
        self.nc = nc
        self.ops = {e: [] for e in ENGS}
        self.c = {e: 0 for e in ENGS}
        self.d = {}

    def last_tok(self, eng):
        return (eng, "c", self.c[eng]) if self.c[eng] > 0 else None

    def emit(self, eng, fn, deps=(), ms=False, dma=False, ch=0):
        tok = None
        if dma:
            key = (eng, ch)
            self.d[key] = self.d.get(key, 0) + 1
            tok = (eng, ("d", ch), self.d[key] * 16)
        elif ms:
            self.c[eng] += 1
            tok = (eng, "c", self.c[eng])
        self.ops[eng].append((fn, [t for t in deps if t is not None], tok))
        return tok

    def build(self, sems):
        nc = self.nc
        with nc.Block() as block:
            def mk(name):
                def body(eng):
                    hw = {}
                    for fn, deps, tok in self.ops[name]:
                        waits = {}
                        for (de, dk, dv) in deps:
                            key = (de, dk)
                            if waits.get(key, -1) < dv:
                                waits[key] = dv
                        for key, dv in waits.items():
                            if hw.get(key, -1) >= dv:
                                continue
                            eng.wait_ge(sems[key], dv)
                            hw[key] = dv
                        inst = fn(eng)
                        if tok is not None:
                            _, k, v = tok
                            inst.then_inc(sems[(name, k)],
                                          16 if isinstance(k, tuple) else 1)
                return body
            for e in ENGS:
                if self.ops[e]:
                    getattr(block, e)(mk(e))


def build_program():
    nc = bass.Bass("TRN2", target_bir_lowering=False, dynamic_dma_scratch_size=16384)

    x_nat = nc.dram_tensor("x_nat", [S, E], F32, kind="ExternalInput").ap()
    xT = nc.dram_tensor("xT", [E, S], F32, kind="ExternalInput").ap()
    WK2 = nc.dram_tensor("WK2", [NCE, 128, E], F32, kind="ExternalInput").ap()
    WQ2 = nc.dram_tensor("WQ2", [NCE, 128, E], F32, kind="ExternalInput").ap()
    WV2 = nc.dram_tensor("WV2", [128, NCE * E], F32, kind="ExternalInput").ap()
    WO2 = nc.dram_tensor("WO2", [NCE, 128, E], F32, kind="ExternalInput").ap()
    BQKV = nc.dram_tensor("BQKV", [1, 3 * E], F32, kind="ExternalInput").ap()
    BO = nc.dram_tensor("BO", [1, E], F32, kind="ExternalInput").ap()
    WG2 = nc.dram_tensor("WG2", [NCH, 128, E], I32, kind="ExternalInput").ap()
    WU2 = nc.dram_tensor("WU2", [NCH, 128, E], I32, kind="ExternalInput").ap()
    WD2 = nc.dram_tensor("WD2", [NCE, 128, HF], I32, kind="ExternalInput").ap()
    xn_scr = nc.dram_tensor("xn_scr", [S, E], BF16, kind="Internal").ap()
    outT = nc.dram_tensor("outT", [E, S], F32, kind="ExternalOutput").ap()

    id_f32_t = nc.inline_tensor(np.eye(128, dtype=np.float32), "id_f32")
    id_bf_t = nc.inline_tensor(np.eye(128, dtype=ml_dtypes.bfloat16), "id_bf")
    ones_row_t = nc.inline_tensor(np.ones((1, 512), dtype=ml_dtypes.bfloat16), "ones_row")
    ones_col_t = nc.inline_tensor(np.ones((128, 1), dtype=ml_dtypes.bfloat16), "ones_col")
    ones_f32r_t = nc.inline_tensor(np.ones((1, 128), dtype=np.float32), "ones_f32r")
    j38row_t = nc.inline_tensor(
        np.tile(np.arange(KSEL, dtype=np.float32)[None, :], (128, 1)), "j38row")
    j38col_t = nc.inline_tensor(
        np.arange(KSEL, dtype=np.float32)[:, None].copy(), "j38col")

    sch = Sch(nc)
    ctx = contextlib.ExitStack()
    with ctx:
        def sb(name, shape, dt):
            return ctx.enter_context(nc.sbuf_tensor(name, shape, dt)).ap()

        xnT = sb("xnT", [128, NCE * S], BF16)
        AR = sb("AR", [128, 2 * NCE * S], BF16)
        kkT = AR[:, 0:NCE * S]
        vna = AR[:, NCE * S:2 * NCE * S]
        XS = sb("XS", [128, 6144], F32)
        xbuf = [XS[:, 0:1024], XS[:, 1024:2048], XS[:, 2048:3072], XS[:, 3072:4096]]
        sqb2 = [XS[:, 4096:5120], XS[:, 5120:6144]]
        R1 = XS[0:1, 0:2048]
        R2 = XS[0:1, 2048:4096]
        R3 = XS[0:1, 4096:6144]
        rmT_row = XS[0:1, 4096:5312]
        xtb = [XS[:, 3072:3584], XS[:, 3584:4096], XS[:, 4096:4608]]
        obuf = [XS[:, 4608:5120], XS[:, 5120:5632], XS[:, 5632:6144]]
        rank_bf = XS[0:1, 0:1024].bitcast(BF16)
        umask_bf = XS[0:1, 1024:2048].bitcast(BF16)
        WBUF = sb("WBUF", [128, NCE * E], BF16)
        WKQ = sb("WKQ", [128, NCE * E], BF16)
        XNB = sb("XNB", [128, 4096], BF16)
        xnb = [XNB[:, 0:1024], XNB[:, 1024:2048], XNB[:, 2048:3072], XNB[:, 3072:4096]]
        HP8 = sb("HP8", [128, 4 * S], BF16)
        SG = sb("SG", [128, 2 * S], BF16)
        sgb = [SG[:, 0:S], SG[:, S:2 * S]]

        def hbuf(hc):
            if hc < 2:
                return XNB[:, hc * S:(hc + 1) * S]
            if hc < 4:
                return WBUF[:, 4096 + (hc - 2) * S:4096 + (hc - 1) * S]
            if hc < 8:
                return HP8[:, (hc - 4) * S:(hc - 3) * S]
            return AR[:, (hc - 8) * S:(hc - 7) * S]

        qtok = sb("qtok", [128, 512], BF16)
        qT = sb("qT", [128, 512], BF16)
        ctx_sb = XS[0:KSEL, 2048:2560].bitcast(BF16)
        ctxT = sb("ctxT", [128, 512], BF16)
        osel = XS[0:KSEL, 2560:3072].bitcast(BF16)
        EXPT = XS[:, 3072:4096].bitcast(BF16)
        expT = [EXPT[:, 0:1024], EXPT[:, 1024:2048]]
        PtA = sb("PtA", [128, NCS * KSEL], BF16)
        PtT = sb("PtT", [KSEL, S], BF16)
        defbf = sb("defbf", [1, E], BF16)
        bqkv = sb("bqkv", [1, 3 * E], BF16)
        bo_bf = sb("bo_bf", [1, E], BF16)
        rmax32 = sb("rmax32", [KSEL, 2 * NH], F32)
        negb = sb("negb", [128, 1], F32)
        rsb = sb("rsb", [KSEL, NH], F32)
        v16 = sb("v16", [16, 128], F32)
        r16 = sb("r16", [16, 128], F32)
        V128 = sb("V128", [128, NCS], F32)
        m8 = sb("m8", [1, 8], F32)
        mcol = sb("mcol", [128, NCS], F32)
        rcol = sb("rcol", [128, NCS], F32)
        bcol = sb("bcol", [128, NCS], F32)
        sxc = sb("sxc", [128, NCS], F32)
        ssc = sb("ssc", [128, NCS], F32)
        tcol = sb("tcol", [128, NCS], F32)
        msq = sb("msq", [128, NCS], F32)
        id_f = sb("id_f", [128, 128], F32)
        id_b = sb("id_b", [128, 128], BF16)
        ones_r = sb("ones_r", [1, 512], BF16)
        ones_c = sb("ones_c", [128, 1], BF16)
        j38r = sb("j38r", [128, KSEL], F32)
        j38c = sb("j38c", [KSEL, 1], F32)
        ones_fr = sb("ones_fr", [1, 128], F32)
        zrow = sb("zrow", [1, 512], BF16)

        psf = ctx.enter_context(nc.psum_tensor("psf", [128, 3584], F32)).ap()
        psb = ctx.enter_context(nc.psum_tensor("psb", [128, 1024], BF16)).ap()

        em = sch.emit

        em("sync", lambda g: g.dma_start(out=id_f, in_=id_f32_t.ap()), dma=True, ch=0)
        em("sync", lambda g: g.dma_start(out=id_b, in_=id_bf_t.ap()), dma=True, ch=0)
        em("sync", lambda g: g.dma_start(out=ones_r, in_=ones_row_t.ap()), dma=True, ch=0)
        em("sync", lambda g: g.dma_start(out=ones_c, in_=ones_col_t.ap()), dma=True, ch=0)
        em("sync", lambda g: g.dma_start(out=j38r, in_=j38row_t.ap()), dma=True, ch=0)
        em("sync", lambda g: g.dma_start(out=j38c, in_=j38col_t.ap()), dma=True, ch=0)
        t_consts = em("sync", lambda g: g.dma_start(out=ones_fr, in_=ones_f32r_t.ap()), dma=True, ch=0)
        t_id_f = t_id_b = t_ones_r = t_ones_c = t_j38r = t_j38c = t_onesf = t_consts
        t_zrow = em("vector", lambda v: v.memset(zrow, 0.0), ms=True)

        em("gpsimd", lambda g: g.dma_start(out=bqkv, in_=BQKV), dma=True, ch=0)
        t_bo = em("gpsimd", lambda g: g.dma_start(out=bo_bf, in_=BO), dma=True, ch=0)
        t_bqkv = t_bo
        t_wkq_h = [None, None]
        for hf in range(2):
            t_wkq_h[hf] = em("gpsimd", lambda g, hf=hf: g.dma_start(
                out=WKQ.rearrange("p (m c) -> p m c", m=NCE)[:, 4 * hf:4 * (hf + 1), :],
                in_=WK2.rearrange("m p c -> p m c")[:, 4 * hf:4 * (hf + 1), :]),
                dma=True, ch=2)
        t_wv_h = [None, None]
        for hf in range(2):
            t_wv_h[hf] = em("gpsimd", lambda g, hf=hf: g.dma_start(
                out=WBUF.rearrange("p (k c) -> p k c", k=NCE)[:, :, hf * 512:(hf + 1) * 512],
                in_=WV2.rearrange("p (k c) -> p k c", k=NCE)[:, :, hf * 512:(hf + 1) * 512]),
                dma=True, ch=1)

        PKB = [0, 512, 1024, 1536, 2048]
        kst = {"tix": 0}
        t_tile_ev = {}
        t_tile_mm = {}

        def emit_kk_tile(m, c, dep_tr):
            tix = kst["tix"]; kst["tix"] += 1
            pc = PKB[tix % 5]
            last = None
            for k in range(NCE):
                last = em("tensor", lambda t, m=m, c=c, k=k, pc=pc: t.matmul(
                    psf[:, pc:pc + 512],
                    WKQ[:, m * E + k * 128:m * E + (k + 1) * 128],
                    xnT[:, k * S + c * 512:k * S + (c + 1) * 512],
                    start=(k == 0), stop=False),
                    deps=([t_wkq_h[m // 4], t_tile_ev.get(tix - 5)] + list(dep_tr)
                          if k == 0 else []))
            last = em("tensor", lambda t, m=m, pc=pc: t.matmul(
                psf[:, pc:pc + 512],
                bqkv[0:1, E + m * 128:E + (m + 1) * 128],
                ones_r, start=False, stop=True),
                deps=[t_bqkv, t_ones_r], ms=True)
            t_tile_mm[tix] = last
            t_tile_ev[tix] = em("scalar", lambda s, m=m, c=c, pc=pc: s.copy(
                out=kkT[:, m * S + c * 512:m * S + (c + 1) * 512],
                in_=psf[:, pc:pc + 512]), deps=[last], ms=True)

        def emit_v_tile(t_, half, dep_tr):
            tix = kst["tix"]; kst["tix"] += 1
            pc = PKB[tix % 5]
            last = None
            for k in range(NCE):
                last = em("tensor", lambda t, t_=t_, half=half, k=k, pc=pc: t.matmul(
                    psf[:, pc:pc + 512],
                    xnT[:, k * S + t_ * 128:k * S + (t_ + 1) * 128],
                    WBUF[:, k * E + half * 512:k * E + (half + 1) * 512],
                    start=(k == 0), stop=False),
                    deps=([t_wv_h[half], t_tile_ev.get(tix - 5)] + list(dep_tr)
                          if k == 0 else []))
            last = em("tensor", lambda t, half=half, pc=pc: t.matmul(
                psf[:, pc:pc + 512],
                ones_r[0:1, 0:128],
                bqkv[0:1, 2 * E + half * 512:2 * E + (half + 1) * 512],
                start=False, stop=True),
                deps=[t_bqkv], ms=True)
            t_tile_mm[tix] = last
            t_tile_ev[tix] = em("scalar", lambda s, t_=t_, half=half, pc=pc: s.copy(
                out=vna[:, t_ * E + half * 512:t_ * E + (half + 1) * 512],
                in_=psf[:, pc:pc + 512]), deps=[last], ms=True)

        t_dx = [None] * NCS
        t_rsx = [None] * NCS
        t_rss = [None] * NCS
        t_xn = [None] * NCS
        t_sq = [None] * NCS
        t_tr = [None] * NCS
        t_scrw = [None] * NCS
        t_trev = [None] * NCS
        t_vcol = [None] * NCS
        t_rcolt = [None] * NCS
        t_bcol = [None] * NCS

        def ln_dx(i):
            t_dx[i] = em("sync", lambda g, i=i: g.dma_start(
                out=xbuf[i % 4], in_=x_nat[i * 128:(i + 1) * 128, :]),
                deps=[t_xn[i - 4] if i >= 4 else None,
                      t_rsx[i - 4] if i >= 4 else None,
                      t_rss[i - 4] if i >= 4 else None], dma=True, ch=1 + i % 4)

        for i in range(2):
            ln_dx(i)
        for it in range(NCS + 2):
            i = it
            if i < NCS:
                sqb = sqb2[i % 2]
                t_sq[i] = em("scalar", lambda s, i=i, sqb=sqb: s.activation(
                    out=sqb, in_=xbuf[i % 4], func=AF.Square),
                    deps=[t_dx[i], t_rss[i - 2] if i >= 2 else None], ms=True)
            if 1 <= it <= NCS:
                j = it - 1
                t_ln = em("scalar", lambda s, j=j: s.activation(
                    out=tcol[:, j:j + 1], in_=V128[:, j:j + 1], func=AF.Ln),
                    deps=[t_vcol[j]], ms=True)
                t_rcolt[j] = em("scalar", lambda s, j=j: s.activation(
                    out=rcol[:, j:j + 1], in_=tcol[:, j:j + 1], func=AF.Exp, scale=-0.5),
                    deps=[t_ln], ms=True)
            if 2 <= it:
                j = it - 2
                t_xn[j] = em("scalar", lambda s, j=j: s.activation(
                    out=xnb[j % 2], in_=xbuf[j % 4], func=AF.Identity,
                    bias=bcol[:, j:j + 1], scale=rcol[:, j:j + 1]),
                    deps=[t_rcolt[j], t_bcol[j], t_dx[j],
                          t_tr[j - 2] if j >= 2 else None,
                          t_scrw[j - 2] if j >= 2 else None], ms=True)
            if i < NCS:
                sqb = sqb2[i % 2]
                t_rsx[i] = em("vector", lambda v, i=i: v.reduce_sum(
                    out=sxc[:, i:i + 1], in_=xbuf[i % 4], axis=AX.X),
                    deps=[t_dx[i]], ms=True)
                t_rss[i] = em("vector", lambda v, i=i, sqb=sqb: v.reduce_sum(
                    out=ssc[:, i:i + 1], in_=sqb, axis=AX.X),
                    deps=[t_sq[i]], ms=True)
                t_m = em("vector", lambda v, i=i: v.tensor_scalar(
                    out=mcol[:, i:i + 1], in0=sxc[:, i:i + 1], scalar1=1.0 / E,
                    scalar2=None, op0=ALU.mult), deps=[t_rsx[i]], ms=True)
                t_msq = em("vector", lambda v, i=i: v.tensor_scalar(
                    out=msq[:, i:i + 1], in0=mcol[:, i:i + 1],
                    scalar1=mcol[:, i:i + 1], scalar2=-EPS,
                    op0=ALU.mult, op1=ALU.add), deps=[t_m], ms=True)
                t_vcol[i] = em("vector", lambda v, i=i: v.scalar_tensor_tensor(
                    out=V128[:, i:i + 1], in0=ssc[:, i:i + 1], scalar=1.0 / E,
                    in1=msq[:, i:i + 1], op0=ALU.mult, op1=ALU.subtract),
                    deps=[t_rss[i], t_msq], ms=True)
            if 1 <= it <= NCS:
                j = it - 1
                t_bcol[j] = em("vector", lambda v, j=j: v.scalar_tensor_tensor(
                    out=bcol[:, j:j + 1], in0=mcol[:, j:j + 1], scalar=-1.0,
                    in1=rcol[:, j:j + 1], op0=ALU.mult, op1=ALU.mult),
                    deps=[t_rcolt[j]], ms=True)
            if 2 <= it:
                j = it - 2
                tr_last = None
                for e in range(NCE):
                    tr_last = em("tensor", lambda t, j=j, e=e: t.transpose(
                        psb[:, e * 128:(e + 1) * 128], xnb[j % 2][:, e * 128:(e + 1) * 128],
                        id_b), deps=[t_xn[j], t_id_b,
                                     t_trev[j - 1] if j >= 1 else None],
                        ms=(e == NCE - 1))
                if j % 2 == 0:
                    t_trev[j] = em("scalar", lambda s, j=j: s.copy(
                        out=xnT.rearrange("p (k s) -> p k s", k=NCE)[:, :, j * 128:(j + 1) * 128],
                        in_=psb.rearrange("p (k s) -> p k s", k=NCE)),
                        deps=[tr_last], ms=True)
                else:
                    t_trev[j] = em("vector", lambda v, j=j: v.tensor_copy(
                        out=xnT.rearrange("p (k s) -> p k s", k=NCE)[:, :, j * 128:(j + 1) * 128],
                        in_=psb.rearrange("p (k s) -> p k s", k=NCE)),
                        deps=[tr_last], ms=True)
                for n in range(2):
                    t_tr[j] = em("tensor", lambda t, j=j, n=n: t.matmul(
                        psf[0:1, 2560 + n * 512:2560 + (n + 1) * 512],
                        ones_c, xnb[j % 2][:, n * 512:(n + 1) * 512],
                        start=(j == 0), stop=(j == NCS - 1)),
                        deps=[t_xn[j], t_ones_c], ms=(n == 1))
                t_scrw[j] = em("sync", lambda g, j=j: g.dma_start(
                    out=xn_scr[j * 128:(j + 1) * 128, :], in_=xnb[j % 2]),
                    deps=[t_xn[j]], dma=True, ch=5 + j % 2)
                if j % 4 == 3:
                    for m in range(NCE):
                        emit_kk_tile(m, j // 4, [t_trev[j], t_trev[j - 1]])
            if 2 <= it + 2 < NCS:
                ln_dx(it + 2)

        t_def = em("vector", lambda v: v.tensor_scalar(
            out=defbf, in0=psf[0:1, 2560:3584], scalar1=1.0 / S, scalar2=None,
            op0=ALU.mult), deps=[t_tr[NCS - 1]], ms=True)

        t_vtr = em("tensor", lambda t: t.transpose(
            psf[0:16, 3072:3200], V128, id_f),
            deps=[t_vcol[NCS - 1], t_id_f, t_def], ms=True)
        t_v16 = em("vector", lambda v: v.tensor_copy(out=v16, in_=psf[0:16, 3072:3200]),
                   deps=[t_vtr, t_def], ms=True)
        t_vrow = em("sync", lambda g: g.dma_start(out=R1, in_=v16),
                    deps=[t_v16, t_xn[NCS - 1], t_rsx[NCS - 1], t_rss[NCS - 1]],
                    dma=True, ch=15)
        prev = t_vrow
        for r in range(5):
            t_mx = em("vector", lambda v: v.max(out=m8, in_=R1), deps=[prev], ms=True)
            if r == 4:
                t_mx = em("vector", lambda v: v.memset(m8[:, 6:8], MINVAL),
                          deps=[t_mx], ms=True)
            prev = em("vector", lambda v: v.match_replace(
                out=R1, in_to_replace=m8, in_values=R1, imm_value=MINVAL),
                deps=[t_mx], ms=True)
        t_mask = em("vector", lambda v: v.tensor_scalar(
            out=R2, in0=R1, scalar1=-1.0e29, scalar2=None, op0=ALU.is_le),
            deps=[prev, t_rss[NCS - 1]], ms=True)
        t_z = em("vector", lambda v: v.memset(R1, 0.0), deps=[t_mask], ms=True)
        t_scan = em("vector", lambda v: v.tensor_tensor_scan(
            out=R3, data0=R2, data1=R1, initial=0.0, op0=ALU.add, op1=ALU.add),
            deps=[t_mask, t_z], ms=True)
        t_rk = em("vector", lambda v: v.scalar_tensor_tensor(
            out=R2, in0=R2, scalar=-51.0, in1=R3, op0=ALU.mult, op1=ALU.add),
            deps=[t_scan], ms=True)
        t_rk = em("vector", lambda v: v.tensor_scalar(
            out=R2, in0=R2, scalar1=50.0, scalar2=None, op0=ALU.add),
            deps=[t_rk], ms=True)
        t_rkbf = em("vector", lambda v: v.tensor_copy(out=rank_bf, in_=R2),
                    deps=[t_rk, t_z], ms=True)
        t_um = em("vector", lambda v: v.tensor_scalar(
            out=umask_bf, in0=R2, scalar1=49.0, scalar2=None, op0=ALU.is_ge),
            deps=[t_rk, t_rkbf], ms=True)
        t_r16 = em("sync", lambda g: g.dma_start(out=r16, in_=R2),
                   deps=[t_rk, t_um], dma=True, ch=15)

        for t_ in range(NCS):
            for half in range(2):
                emit_v_tile(t_, half, [t_trev[t_]])
        t_all_mm = t_tile_mm[kst["tix"] - 1]
        t_pA = em("vector", lambda v: v.engine_nop(),
                  deps=[t_tile_ev[kst["tix"] - 1], t_tile_ev[kst["tix"] - 2]],
                  ms=True)
        t_kk_all_mm = t_tile_mm[31]
        t_v_all_mm = t_all_mm
        t_kk_all_ev = t_v_all_ev = t_pA

        t_ptT = [None] * 4
        prev_ev = t_v16
        for c in range(4):
            t_mm = em("tensor", lambda t, c=c: t.matmul(
                psf[0:KSEL, 3072:3584], ones_r[0:1, 0:KSEL],
                rank_bf[0:1, c * 512:(c + 1) * 512], start=True, stop=True),
                deps=[t_rkbf, t_ones_r, prev_ev], ms=True)
            prev_ev = em("vector", lambda v, c=c: v.tensor_scalar(
                out=PtT[:, c * 512:(c + 1) * 512], in0=psf[0:KSEL, 3072:3584],
                scalar1=j38c, scalar2=None, op0=ALU.is_equal),
                deps=[t_mm, t_j38c], ms=True)
            t_ptT[c] = prev_ev
        t_rtr = em("tensor", lambda t: t.transpose(
            psf[:, 3072:3088], r16, id_f[0:16, 0:16]),
            deps=[t_r16, prev_ev], ms=True)
        t_r128 = em("vector", lambda v: v.tensor_copy(
            out=V128, in_=psf[:, 3072:3088]), deps=[t_rtr], ms=True)
        t_pta = [None] * NCS
        for t_ in range(NCS):
            t_pta[t_] = em("vector", lambda v, t_=t_: v.tensor_scalar(
                out=PtA[:, t_ * KSEL:(t_ + 1) * KSEL], in0=j38r,
                scalar1=V128[:, t_:t_ + 1], scalar2=None, op0=ALU.is_equal),
                deps=[t_r128, t_j38r], ms=True)

        t_kn = [None] * NCE
        t_sqk = [None] * NCE

        def emit_kn(m):
            if m % 2 == 0:
                t_sqk[m] = em("scalar", lambda s, m=m: s.activation(
                    out=SG[:, 0:S], in_=kkT[:, m * S:(m + 1) * S], func=AF.Square),
                    deps=[t_kk_all_ev, t_kn[m - 1] if m >= 1 else None], ms=True)
            else:
                t_sqk[m] = em("vector", lambda v, m=m: v.tensor_tensor(
                    out=SG[:, 0:S], in0=kkT[:, m * S:(m + 1) * S],
                    in1=kkT[:, m * S:(m + 1) * S], op=ALU.mult),
                    deps=[t_kk_all_ev, t_kn[m - 1] if m >= 1 else None], ms=True)
            last = None
            for cc in range(4):
                last = em("tensor", lambda t, m=m, cc=cc: t.matmul(
                    psf[0:1, cc * 512:(cc + 1) * 512],
                    ones_c, SG[:, cc * 512:(cc + 1) * 512],
                    start=(m == 0), stop=(m == NCE - 1)),
                    deps=[t_sqk[m], t_v_all_ev] if cc == 0 else [],
                    ms=(cc == 3))
            t_kn[m] = last

        t_gz = em("tensor", lambda t: t.matmul(
            psf[:, 3072:3584], ones_r[0:1, 0:128], zrow,
            start=True, stop=False),
            deps=[t_zrow, t_r128, t_v_all_ev], ms=True)
        t_xnr = [None] * NCS
        t_gmm = [None] * NCS
        for i in range(NCS):
            b = i % 4
            t_xnr[i] = em("sync", lambda g, i=i, b=b: g.dma_start(
                out=xnb[b], in_=xn_scr[i * 128:(i + 1) * 128, :]),
                deps=[t_scrw[NCS - 1], t_scrw[NCS - 2],
                      t_gmm[i - 4] if i >= 4 else None,
                      (t_tr[14 + i] if i < 2 else None)],
                dma=True, ch=7 + i % 4)
            last = None
            for e in range(NCE):
                last = em("tensor", lambda t, i=i, b=b, e=e: t.matmul(
                    psf[:, 3072 + e * 64:3072 + e * 64 + KSEL],
                    xnb[b][:, e * 128:(e + 1) * 128],
                    PtA[:, i * KSEL:(i + 1) * KSEL],
                    start=False,
                    stop=(i == NCS - 1)),
                    deps=[t_xnr[i], t_pta[i], t_r128, t_v_all_ev],
                    ms=(e == NCE - 1))
            t_gmm[i] = last
            if i % 2 == 1 and i // 2 < NCE:
                emit_kn(i // 2)
        t_qev = em("vector", lambda v: v.tensor_copy(
            out=qtok.rearrange("p (m j) -> p m j", m=NCE)[:, :, 0:KSEL],
            in_=psf[:, 3072:3584].rearrange("p (m j) -> p m j", m=NCE)[:, :, 0:KSEL]),
            deps=[t_gmm[NCS - 1]], ms=True)

        t_wqf = em("gpsimd", lambda g: g.dma_start(
            out=WKQ.rearrange("p (m c) -> p m c", m=NCE),
            in_=WQ2.rearrange("m p c -> p m c")),
            deps=[t_kk_all_mm], dma=True, ch=2)
        t_qTmm = [None] * NCE
        for m in range(NCE):
            last = None
            for k in range(NCE):
                last = em("tensor", lambda t, m=m, k=k: t.matmul(
                    psf[:, 3072 + m * 64:3072 + m * 64 + KSEL],
                    WKQ[:, m * E + k * 128:m * E + (k + 1) * 128],
                    qtok[:, k * 64:k * 64 + KSEL],
                    start=(k == 0), stop=False),
                    deps=[t_wqf, t_qev])
            last = em("tensor", lambda t, m=m: t.matmul(
                psf[:, 3072 + m * 64:3072 + m * 64 + KSEL],
                bqkv[0:1, m * 128:(m + 1) * 128],
                ones_r[0:1, 0:KSEL], start=False, stop=(m == NCE - 1)),
                deps=[t_bqkv], ms=True)
            t_qTmm[m] = last
        t_qTev = em("vector", lambda v: v.tensor_copy(
            out=qT.rearrange("p (m j) -> p m j", m=NCE)[:, :, 0:KSEL],
            in_=psf[:, 3072:3584].rearrange("p (m j) -> p m j", m=NCE)[:, :, 0:KSEL]),
            deps=[t_qTmm[NCE - 1]], ms=True)
        t_qpad = em("vector", lambda v: v.memset(
            qT.rearrange("p (m j) -> p m j", m=NCE)[:, :, KSEL:64], 0.0),
            deps=[t_qTev], ms=True)

        t_kn2 = em("vector", lambda v: v.reduce_max(
            out=negb[0:1, 0:1], in_=psf[0:1, 0:2048], axis=AX.X),
            deps=[t_kn[NCE - 1]], ms=True)
        t_sqq = em("scalar", lambda s: s.activation(
            out=SG[:, 0:512], in_=qT, func=AF.Square),
            deps=[t_qpad, t_qTev, t_kn[NCE - 1]], ms=True)
        t_qn = em("tensor", lambda t: t.matmul(
            psf[0:1, 3072:3584], ones_c, SG[:, 0:512], start=True, stop=True),
            deps=[t_sqq, t_qTmm[NCE - 1], t_qev], ms=True)
        t_qn2 = em("vector", lambda v: v.reduce_sum(
            out=v16[0:1, 0:64],
            in_=psf[0:1, 3072:3584].rearrange("a (m j) -> a j m", m=NCE),
            axis=AX.X), deps=[t_qn], ms=True)
        t_qn3 = em("vector", lambda v: v.reduce_max(
            out=v16[0:1, 64:65], in_=v16[0:1, 0:KSEL], axis=AX.X),
            deps=[t_qn2], ms=True)
        t_b2 = em("vector", lambda v: v.tensor_tensor(
            out=v16[0:1, 65:66], in0=v16[0:1, 64:65], in1=negb[0:1, 0:1],
            op=ALU.mult), deps=[t_qn3, t_kn2], ms=True)
        t_bsq = em("scalar", lambda s: s.activation(
            out=v16[0:1, 66:67], in_=v16[0:1, 65:66], func=AF.Sqrt),
            deps=[t_b2], ms=True)
        t_negb8 = em("vector", lambda v: v.tensor_scalar(
            out=v16[0:1, 67:68], in0=v16[0:1, 66:67], scalar1=-1.02 / 8.0,
            scalar2=None, op0=ALU.mult), deps=[t_bsq], ms=True)
        t_bb = em("tensor", lambda t: t.matmul(
            psf[:, 3110:3111], ones_fr, v16[0:1, 67:68], start=True, stop=True),
            deps=[t_negb8, t_onesf, t_qn2], ms=True)
        t_negb = em("vector", lambda v: v.tensor_copy(
            out=negb, in_=psf[:, 3110:3111]), deps=[t_bb], ms=True)

        SCB = [2048, 2560, 2048 + 256, 2560 + 256]

        def sc_col(c):
            g = c // 4
            return 2048 + (g % 2) * 512 + (g // 2) * 256 + (c % 4) * 64

        t_exp = {}
        t_ctxev = [None] * NH
        t_sslast = [None] * NH
        t_ctxlast = [None] * NH
        for h in range(NH):
            po = (h % 2) * 64
            mo = h // 2
            eb = expT[h % 2]
            t_ss = t_cx = None
            for g in range(4):
                for c in range(4 * g, 4 * g + 4):
                    em("tensor", lambda t, h=h, c=c, po=po, mo=mo: t.matmul(
                        psf[:, sc_col(c):sc_col(c) + KSEL],
                        kkT[po:po + 64, mo * S + c * 128:mo * S + (c + 1) * 128],
                        qT[po:po + 64, mo * 64:mo * 64 + KSEL],
                        start=True, stop=True),
                        deps=[t_exp.get((h, g - 2)) or t_exp.get((h - 1, 2 + g)),
                              t_ctxlast[h - 2] if h >= 2 else None,
                              t_sslast[h - 2] if h >= 2 else None,
                              t_def if h == 0 else None,
                              t_v_all_ev if h == 0 else None],
                        ms=(c % 4 == 3))
                gcol = 2048 + (g % 2) * 512 + (g // 2) * 256
                t_exp[(h, g)] = em("scalar", lambda s, h=h, g=g, eb=eb, gcol=gcol:
                    s.activation(
                        out=eb[:, g * 256:(g + 1) * 256].rearrange(
                            "p (c j) -> p c j", c=4)[:, :, 0:KSEL],
                        in_=psf[:, gcol:gcol + 256].rearrange(
                            "p (c j) -> p c j", c=4)[:, :, 0:KSEL],
                        func=AF.Exp, scale=0.125, bias=negb),
                    deps=[sch.last_tok("tensor"), t_negb,
                          t_sslast[h - 2] if h >= 2 else None,
                          t_ctxlast[h - 2] if h >= 2 else None], ms=True)
                if g >= 1:
                    for c in range(4 * (g - 1), 4 * g):
                        t_ss = em("tensor", lambda t, h=h, c=c, eb=eb: t.matmul(
                            psf[0:KSEL, 3072 + 2 * h:3072 + 2 * h + 1],
                            eb[:, c * 64:c * 64 + KSEL], ones_c,
                            start=(c == 0), stop=False),
                            deps=[t_exp[(h, c // 4)],
                                  t_negb if (h == 0 and c == 0) else None])
                        t_cx = em("tensor", lambda t, h=h, c=c, eb=eb: t.matmul(
                            psf[0:KSEL, 1024 + h * 64:1024 + (h + 1) * 64],
                            eb[:, c * 64:c * 64 + KSEL],
                            vna[:, c * E + h * 64:c * E + (h + 1) * 64],
                            start=(c == 0), stop=False),
                            deps=[t_exp[(h, c // 4)],
                                  t_kn2 if (h == 0 and c == 0) else None])
            for c in range(12, 16):
                t_ss = em("tensor", lambda t, h=h, c=c, eb=eb: t.matmul(
                    psf[0:KSEL, 3072 + 2 * h:3072 + 2 * h + 1],
                    eb[:, c * 64:c * 64 + KSEL], ones_c,
                    start=False, stop=(c == NCS - 1)),
                    deps=[t_exp[(h, 3)]], ms=(c == NCS - 1))
                t_cx = em("tensor", lambda t, h=h, c=c, eb=eb: t.matmul(
                    psf[0:KSEL, 1024 + h * 64:1024 + (h + 1) * 64],
                    eb[:, c * 64:c * 64 + KSEL],
                    vna[:, c * E + h * 64:c * E + (h + 1) * 64],
                    start=False, stop=(c == NCS - 1)),
                    deps=[t_exp[(h, 3)]], ms=(c == NCS - 1))
            t_sslast[h] = t_ss
            t_ctxlast[h] = t_cx
        em("scalar", lambda s: s.activation(
            out=tcol[:, 0:1], in_=tcol[:, 0:1], func=AF.Silu),
            deps=[t_exp[(NH - 1, 3)]], ms=True)
        for h in range(NH):
            t_rc = em("vector", lambda v, h=h: v.reciprocal(
                out=rsb[:, h:h + 1], in_=psf[0:KSEL, 3072 + 2 * h:3072 + 2 * h + 1]),
                deps=[t_sslast[NH - 1], t_ctxlast[NH - 1]], ms=True)
            t_ctxev[h] = em("scalar", lambda s, h=h: s.activation(
                out=ctx_sb[:, h * 64:(h + 1) * 64],
                in_=psf[0:KSEL, 1024 + h * 64:1024 + (h + 1) * 64],
                func=AF.Identity, scale=rsb[:, h:h + 1]),
                deps=[t_ctxlast[NH - 1], t_sslast[NH - 1], t_rc], ms=True)

        tr_last = None
        for k in range(NCE):
            tr_last = em("tensor", lambda t, k=k: t.transpose(
                psb[:, k * 64:k * 64 + KSEL],
                ctx_sb[:, k * 128:(k + 1) * 128], id_b[0:KSEL, 0:KSEL]),
                deps=[t_ctxev[2 * k], t_ctxev[2 * k + 1],
                      t_trev[NCS - 1]], ms=(k == NCE - 1))
        t_ctxT = em("vector", lambda v: v.tensor_copy(
            out=ctxT, in_=psb[:, 0:512]), deps=[tr_last], ms=True)
        t_wo = em("gpsimd", lambda g: g.dma_start(
            out=WKQ.rearrange("p (k j) -> p k j", k=NCE),
            in_=WO2.rearrange("k p j -> p k j")),
            deps=[t_qTmm[NCE - 1]], dma=True, ch=1)
        t_oselmm = [None] * 2
        for n in range(2):
            last = None
            for k in range(NCE):
                last = em("tensor", lambda t, n=n, k=k: t.matmul(
                    psf[0:KSEL, 2048 + n * 512:2048 + (n + 1) * 512],
                    ctxT[:, k * 64:k * 64 + KSEL],
                    WKQ[:, k * E + n * 512:k * E + (n + 1) * 512],
                    start=(k == 0), stop=False),
                    deps=[t_ctxT, t_wo,
                          t_sslast[NH - 1], t_ctxlast[NH - 1]])
            last = em("tensor", lambda t, n=n: t.matmul(
                psf[0:KSEL, 2048 + n * 512:2048 + (n + 1) * 512],
                ones_r[0:1, 0:KSEL], bo_bf[0:1, n * 512:(n + 1) * 512],
                start=False, stop=True), deps=[t_bo], ms=True)
            t_oselmm[n] = last
        t_oselev = em("scalar", lambda s: s.copy(out=osel, in_=psf[0:KSEL, 2048:3072]),
                      deps=[t_oselmm[0], t_oselmm[1]], ms=True)

        t_stprev = {}
        t_dnall = {}
        t_wg = {}
        t_wu = {}
        t_silu = {}
        t_hp = {}
        t_glast = {}
        t_ulast = {}
        t_wd = {}
        t_dnev = {}
        t_f2last = {}

        def h_war(hc, cc):
            if cc != 0:
                return []
            if hc == 0:
                return [t_gmm[NCS - 1], t_qev, t_oselev]
            if hc == 1:
                return [t_gmm[NCS - 1]]
            if hc == 2:
                return [t_v_all_mm]
            if hc == 8:
                return [t_sslast[NH - 1], t_kn[NCE - 1]]
            if hc == 16:
                return [t_ctxlast[NH - 1]]
            return []

        for hc in range(NCH):
            wslot = hc % 2
            t_wg[hc] = em("gpsimd", lambda g, hc=hc, wslot=wslot:
                g.dma_start(out=WBUF[:, wslot * 1024:(wslot + 1) * 1024],
                            in_=WG2[hc]),
                deps=[t_glast.get(hc - 2),
                      (t_v_all_mm if hc < 2 else None)],
                dma=True, ch=3 + hc % 2)
            t_wu[hc] = em("gpsimd", lambda g, hc=hc, wslot=wslot:
                g.dma_start(out=WBUF[:, 2048 + wslot * 1024:2048 + (wslot + 1) * 1024],
                            in_=WU2[hc]),
                deps=[t_ulast.get(hc - 2),
                      (t_v_all_mm if hc < 2 else None)],
                dma=True, ch=5 + hc % 2)
            gl = {}
            for cc in range(4):
                p2 = (hc, cc - 2) if cc >= 2 else (hc - 1, cc + 2)
                gco = (cc % 2) * 512
                uco = 1024 + (cc % 2) * 512
                glast = None
                for k in range(NCE):
                    glast = em("tensor", lambda t, hc=hc, cc=cc, k=k,
                               wslot=wslot, gco=gco: t.matmul(
                        psf[:, gco:gco + 512],
                        WBUF[:, wslot * 1024 + k * 128:wslot * 1024 + (k + 1) * 128],
                        xnT[:, k * S + cc * 512:k * S + (cc + 1) * 512],
                        start=(k == 0), stop=(k == NCE - 1)),
                        deps=([t_wg[hc], t_silu.get(p2)] +
                              ([t_exp[(NH - 1, 3)], t_kn2] if (hc == 0 and cc < 2) else [])
                              if k == 0 else []),
                        ms=(k == NCE - 1))
                gl[cc] = glast
                if cc == 3:
                    t_glast[hc] = glast
                ulast = None
                for k in range(NCE):
                    ulast = em("tensor", lambda t, hc=hc, cc=cc, k=k,
                               wslot=wslot, uco=uco: t.matmul(
                        psf[:, uco:uco + 512],
                        WBUF[:, 2048 + wslot * 1024 + k * 128:2048 + wslot * 1024 + (k + 1) * 128],
                        xnT[:, k * S + cc * 512:k * S + (cc + 1) * 512],
                        start=(k == 0), stop=(k == NCE - 1)),
                        deps=([t_wu[hc], t_hp.get(p2)] +
                              ([t_ctxev[NH - 1], t_kn2] if (hc == 0 and cc < 2) else [])
                              if k == 0 else []),
                        ms=(k == NCE - 1))
                if cc == 3:
                    t_ulast[hc] = ulast
                t_silu[(hc, cc)] = em("scalar",
                    lambda s, hc=hc, cc=cc, gco=gco: s.activation(
                        out=sgb[hc % 2][:, cc * 512:(cc + 1) * 512],
                        in_=psf[:, gco:gco + 512],
                        func=AF.Silu, scale=QSC),
                    deps=[gl[cc], ulast, t_hp.get((hc - 2, cc)),
                          (t_kn[NCE - 1] if hc == 0 else None),
                          (t_qn if hc == 0 else None)], ms=True)
                t_hp[(hc, cc)] = em("vector",
                    lambda v, hc=hc, cc=cc, uco=uco: v.scalar_tensor_tensor(
                        out=hbuf(hc)[:, cc * 512:(cc + 1) * 512],
                        in0=psf[:, uco:uco + 512],
                        scalar=QSC * QSC,
                        in1=sgb[hc % 2][:, cc * 512:(cc + 1) * 512],
                        op0=ALU.mult, op1=ALU.mult),
                    deps=[t_silu[(hc, cc)], ulast] + h_war(hc, cc),
                    ms=True)
        for m in range(NCE):
            wdo = (m % 2) * 3072
            t_wd[m] = em("gpsimd", lambda g, m=m, wdo=wdo: g.dma_start(
                    out=WKQ[:, wdo:wdo + 3072], in_=WD2[m]),
                deps=[t_f2last.get((m - 2, 3)),
                      (t_oselmm[1] if m < 2 else None)],
                dma=True, ch=7 + m % 2)
            for cc in range(4):
                tile_i = m * 4 + cc
                par = tile_i % 3
                pcol = 2048 + (tile_i % 3) * 512
                last = None
                for hc in range(NCH):
                    last = em("tensor", lambda t, m=m, cc=cc, hc=hc,
                              pcol=pcol, wdo=wdo: t.matmul(
                        psf[:, pcol:pcol + 512],
                        WKQ[:, wdo + hc * 128:wdo + (hc + 1) * 128],
                        hbuf(hc)[:, cc * 512:(cc + 1) * 512],
                        start=(hc == 0), stop=False),
                        deps=([t_wd[m], t_hp[(NCH - 1, cc)],
                               t_dnall.get(tile_i - 3)]
                              if hc == 0 else []))
                last = em("tensor", lambda t, m=m, cc=cc, pcol=pcol: t.matmul(
                    psf[:, pcol:pcol + 512],
                    osel[:, m * 128:(m + 1) * 128],
                    PtT[:, cc * 512:(cc + 1) * 512],
                    start=False, stop=False),
                    deps=[t_oselev, t_ptT[cc]])
                last = em("tensor", lambda t, m=m, cc=cc, pcol=pcol: t.matmul(
                    psf[:, pcol:pcol + 512],
                    defbf[0:1, m * 128:(m + 1) * 128],
                    umask_bf[0:1, cc * 512:(cc + 1) * 512],
                    start=False, stop=True),
                    deps=[t_def, t_um], ms=True)
                t_f2last[(m, cc)] = last
                t_xt = em("sync", lambda g, m=m, cc=cc, par=par: g.dma_start(
                    out=xtb[par],
                    in_=xT[m * 128:(m + 1) * 128, cc * 512:(cc + 1) * 512]),
                    deps=[t_um if tile_i == 0 else None,
                          t_r16 if tile_i == 0 else None,
                          t_ctxlast[NH - 1] if tile_i < 2 else None,
                          t_dnall.get(tile_i - 3)],
                    dma=True, ch=11 + par)
                t_dnev[(m, cc)] = em("vector",
                    lambda v, par=par, pcol=pcol: v.tensor_tensor(
                        out=obuf[par], in0=psf[:, pcol:pcol + 512],
                        in1=xtb[par], op=ALU.add),
                    deps=[last, t_xt, t_stprev.get(par)], ms=True)
                t_dnall[tile_i] = t_dnev[(m, cc)]
                if tile_i < NCE * 4 - 2:
                    t_stprev[par] = em("sync", lambda g, m=m, cc=cc, par=par:
                        g.dma_start(
                            out=outT[m * 128:(m + 1) * 128, cc * 512:(cc + 1) * 512],
                            in_=obuf[par]),
                        deps=[t_dnev[(m, cc)]], dma=True, ch=13 + par)
                else:
                    chs = (5, 6, 7, 10) if tile_i == NCE * 4 - 2 else (1, 2, 3, 4)
                    t_stprev[par] = em("sync", lambda g: g.nop(),
                        deps=[em("sync", lambda g, m=m, cc=cc, par=par, sp=sp:
                            g.dma_start(
                                out=outT[m * 128:(m + 1) * 128,
                                         cc * 512 + sp * 128:cc * 512 + (sp + 1) * 128],
                                in_=obuf[par][:, sp * 128:(sp + 1) * 128]),
                            deps=[t_dnev[(m, cc)]], dma=True, ch=chs[sp])
                            for sp in range(4)], ms=True)

        em("sync", lambda g: g.nop(),
           deps=[t_stprev.get(0), t_stprev.get(1), t_stprev.get(2)])

        sems = {}
        for e in ENGS:
            sems[(e, "c")] = ctx.enter_context(nc.semaphore(f"s_{e}_c"))
        for (e, chn) in sorted(sch.d):
            sems[(e, ("d", chn))] = ctx.enter_context(nc.semaphore(f"s_{e}_d{chn}"))
        sch.build(sems)
    return nc


_NC_CACHE = {}


def _get_nc():
    if "nc" not in _NC_CACHE:
        _NC_CACHE["nc"] = build_program()
    return _NC_CACHE["nc"]


def kernel(L0_attn_x, L0_attn_in_proj_w, L0_attn_in_proj_b, L0_attn_out_w,
           L0_attn_out_b, L0_ffn_w_gate, L0_ffn_w_up, L0_ffn_w_down):
    x = np.asarray(L0_attn_x)
    ipw = np.asarray(L0_attn_in_proj_w)
    ipb = np.asarray(L0_attn_in_proj_b)
    ow = np.asarray(L0_attn_out_w)
    ob = np.asarray(L0_attn_out_b)
    wg = np.asarray(L0_ffn_w_gate)
    wu = np.asarray(L0_ffn_w_up)
    wd = np.asarray(L0_ffn_w_down)

    Wq, Wk, Wv = ipw[0:E], ipw[E:2 * E], ipw[2 * E:3 * E]
    def blk_kM(W):
        return np.ascontiguousarray(
            W.reshape(NCE, 128, NCE, 128).transpose(0, 3, 2, 1).reshape(NCE, 128, E))
    WK2 = blk_kM(Wk)
    WQ2 = blk_kM(Wq)
    WV2 = np.ascontiguousarray(
        Wv.reshape(E, NCE, 128).transpose(2, 1, 0).reshape(128, NCE * E))
    WO2 = np.ascontiguousarray(
        ow.reshape(E, NCE, 128).transpose(1, 2, 0).reshape(NCE, 128, E))
    WG2 = np.ascontiguousarray(
        wg.reshape(NCH, 128, NCE, 128).transpose(0, 3, 2, 1).reshape(NCH, 128, E))
    WU2 = np.ascontiguousarray(
        wu.reshape(NCH, 128, NCE, 128).transpose(0, 3, 2, 1).reshape(NCH, 128, E))
    WD2 = np.ascontiguousarray(
        wd.reshape(NCE, 128, NCH, 128).transpose(0, 3, 2, 1).reshape(NCE, 128, HF))
    BQKV = np.ascontiguousarray(ipb.reshape(1, 3 * E))
    BOa = np.ascontiguousarray(ob.reshape(1, E))

    nc = _get_nc()
    in_maps = []
    for b in range(8):
        in_maps.append(dict(
            x_nat=np.ascontiguousarray(x[b]),
            xT=np.ascontiguousarray(x[b].T),
            WK2=WK2, WQ2=WQ2, WV2=WV2, WO2=WO2, BQKV=BQKV, BO=BOa,
            WG2=WG2, WU2=WU2, WD2=WD2,
        ))
    res = run_bass_kernel_spmd(nc, in_maps, core_ids=list(range(8)))
    out = np.stack([np.asarray(res.results[b]["outT"]).T for b in range(8)], axis=0)
    return np.ascontiguousarray(out, dtype=np.float32)
